# version 19
# speedup vs baseline: 1.4593x; 1.0019x over previous
import numpy as np
from contextlib import ExitStack

import concourse.bass as bass
import concourse.mybir as mybir
from concourse.bass_utils import run_bass_kernel_spmd

F32 = mybir.dt.float32
BF16 = mybir.dt.bfloat16
AF = mybir.ActivationFunctionType
OP = mybir.AluOpType
AX = mybir.AxisListType

ENGS = ("sp", "pe", "act", "dve", "pool")
CONSTS = [0.0, 1e-05, -0.5, 1.0, 0.5, 2.0, -1.0, 80.0]
BIG = 1 << 60


class Cfg:
    def __init__(self, T=4096, C=1024, DI=2048, TC=512):
        self.T = T
        self.C = C
        self.DI = DI
        self.DMY = DI
        self.NST = 16
        self.DTR = 64
        self.CK = C // 128
        self.DMK = self.DMY // 128
        self.TC = TC
        self.NTC = T // TC
        self.EPS = 1e-5


class Op:
    __slots__ = ("eng", "fn", "deps", "need_inc", "val", "step", "tag", "sem", "prewait")

    def __init__(self, eng, fn, step, tag):
        self.eng = eng
        self.fn = fn
        self.deps = []
        self.need_inc = False
        self.val = None
        self.step = step
        self.tag = tag


class Prog:
    def __init__(self, nc):
        self.nc = nc
        self.q = {e: [] for e in ENGS}
        self.wr = {}
        self.rd = {}

    @staticmethod
    def R(h, lo=0, hi=BIG):
        name = h if isinstance(h, str) else h.name
        return (name, lo, hi)

    def _dep(self, op, rs, ws):
        deps = {}
        for (n, lo, hi) in rs:
            for (l2, h2, p) in self.wr.get(n, ()):
                if lo < h2 and l2 < hi and p is not op:
                    deps[id(p)] = p
        for (n, lo, hi) in ws:
            for (l2, h2, p) in self.wr.get(n, ()):
                if lo < h2 and l2 < hi and p is not op:
                    deps[id(p)] = p
            for (l2, h2, p) in self.rd.get(n, ()):
                if lo < h2 and l2 < hi and p is not op:
                    deps[id(p)] = p
        op.deps = list(deps.values())
        for (n, lo, hi) in ws:
            wl = self.wr.setdefault(n, [])
            wl[:] = [e for e in wl if not (lo <= e[0] and e[1] <= hi)]
            wl.append((lo, hi, op))
            rl = self.rd.setdefault(n, [])
            rl[:] = [e for e in rl if not (lo <= e[0] and e[1] <= hi)]
        for (n, lo, hi) in rs:
            self.rd.setdefault(n, []).append((lo, hi, op))

    def emit(self, eng, fn, reads=(), writes=(), step=1, tag=""):
        op = Op(eng, fn, step, tag)
        self._dep(op, reads, writes)
        self.q[eng].append(op)
        return op

    NDMASEM = 8

    def finalize(self, es):
        nc = self.nc
        sems = {}
        for e in ENGS:
            sems[e] = es.enter_context(nc.semaphore(f"sem_{e}"))
        dsems = {}
        for e in ("sp", "pool", "act", "dve", "pe"):
            dsems[e] = [es.enter_context(nc.semaphore(f"dsem_{e}{i}"))
                        for i in range(self.NDMASEM if e == "sp" else 16)]
        for ops in self.q.values():
            for op in ops:
                if op.step == 16:
                    op.need_inc = True
                for d in op.deps:
                    d.need_inc = True
        cnt = {}
        drot = {e: 0 for e in dsems}
        for e in ENGS:
            for op in self.q[e]:
                if not op.need_inc:
                    op.sem = None
                    continue
                if op.step == 16:
                    i = drot[e] % len(dsems[e])
                    drot[e] += 1
                    sem = dsems[e][i]
                else:
                    sem = sems[e]
                key = sem.name if hasattr(sem, "name") else id(sem)
                prev = cnt.get(key, 0)
                op.prewait = (sem, prev) if op.step == 16 and prev else None
                cnt[key] = prev + op.step
                op.sem = sem
                op.val = cnt[key]

        def body(eng_name):
            def f(h):
                waited = {}
                for op in self.q[eng_name]:
                    need = {}
                    for d in op.deps:
                        k = d.sem.name if hasattr(d.sem, "name") else id(d.sem)
                        if need.get(k, (None, 0))[1] < d.val:
                            need[k] = (d.sem, d.val)
                    if op.sem is not None and getattr(op, "prewait", None):
                        sem, v = op.prewait
                        k = sem.name if hasattr(sem, "name") else id(sem)
                        if need.get(k, (None, 0))[1] < v:
                            need[k] = (sem, v)
                    for k, (sem, v) in need.items():
                        if waited.get(k, 0) < v:
                            h.wait_ge(sem, v)
                            waited[k] = v
                    inst = op.fn(h)
                    if op.sem is not None:
                        if inst is None:
                            raise RuntimeError(f"op {op.tag} returned no inst")
                        inst.then_inc(op.sem, op.step)
            return f

        with nc.Block() as blk:
            blk.sync(body("sp"))
            blk.tensor(body("pe"))
            blk.scalar(body("act"))
            blk.vector(body("dve"))
            blk.gpsimd(body("pool"))


class Builder:
    def __init__(self, cfg: Cfg, dbg=False):
        self.dbg = dbg
        self.cfg = cfg
        self.nc = bass.Bass("TRN2", target_bir_lowering=False, num_devices=8)
        self.es = ExitStack()
        self.p = Prog(self.nc)
        self.guards = {}
        self.psum = []
        self.psum_i = 0
        self._uid = 0

    def uid(self):
        self._uid += 1
        return self._uid

    SBUF_BASE = 16512
    SBUF_BYTES = 212480

    def sb(self, name, shape, dtype):
        per_part = int(np.prod(shape[1:])) * mybir.dt.size(dtype)
        size = (per_part + 31) // 32 * 32
        if not hasattr(self, "_freelist"):
            self._freelist = [[self.SBUF_BASE, self.SBUF_BASE + self.SBUF_BYTES, []]]
        small = size <= 8192
        fits = [seg for seg in self._freelist if seg[1] - seg[0] >= size]
        if fits:
            seg = min(fits, key=lambda s: s[1] - s[0])
            if small:
                off = seg[1] - size
                seg[1] -= size
            else:
                off = seg[0]
                seg[0] += size
            ghosts = list(seg[2])
        else:
            used = self.SBUF_BYTES - sum(sg[1] - sg[0] for sg in self._freelist)
            raise RuntimeError(f"SBUF OOM: want {size}, used {used} ({name})")
        
        h = self.nc.alloc_sbuf_tensor_at(name, list(shape), dtype, offset=off)
        self.guards[h.name] = (off, size)
        if ghosts:
            self.p.wr[h.name] = [(0, BIG, g) for g in ghosts]
        return h

    def free(self, *handles):
        for h in handles:
            off, size = self.guards.pop(h.name)
            deps = {}
            for (_, _, op) in self.p.wr.pop(h.name, ()):
                deps[id(op)] = op
            for (_, _, op) in self.p.rd.pop(h.name, ()):
                deps[id(op)] = op
            self._freelist.append([off, off + size, list(deps.values())])
        self._freelist.sort()
        merged = []
        for seg in self._freelist:
            if merged and merged[-1][1] >= seg[0]:
                merged[-1][1] = max(merged[-1][1], seg[1])
                merged[-1][2] = merged[-1][2] + seg[2]
            else:
                merged.append(seg)
        self._freelist = merged

    def dram(self, name, shape, dtype, kind="Internal"):
        return self.nc.dram_tensor(name, shape, dtype, kind=kind)

    def mk_psum(self, n=8):
        for i in range(n):
            g = self.nc.psum_tensor(f"pb{i}", [128, 512], F32)
            h = g.__enter__()
            self.guards[f"pb{i}"] = g
            self.psum.append(h)

    def nps(self):
        h = self.psum[self.psum_i % len(self.psum)]
        self.psum_i += 1
        return h

    def dma(self, out, in_, reads, writes, q="sp", tag="dma", slow=False):
        return self.p.emit(
            q,
            lambda e, o=out, i=in_, s=slow:
                e.dma_start(out=o, in_=i, allow_slow_non_contiguous=True) if s
                else e.dma_start(out=o, in_=i),
            reads, writes, step=16, tag=tag)

    def mmacc(self, out, pieces, reads, writes, tag="mm"):
        def f(e, o=out, ps=tuple(pieces)):
            n = len(ps)
            inst = None
            for i, (l, r) in enumerate(ps):
                inst = e.matmul(out=o, lhsT=l, rhs=r,
                                start=(i == 0), stop=(i == n - 1))
            return inst
        return self.p.emit("pe", f, reads, writes, tag=tag)

    def tr(self, out, in_, ident, reads, writes, tag="tr"):
        return self.p.emit(
            "pe",
            lambda e, o=out, i=in_, d=ident: e.transpose(out=o, in_=i, identity=d),
            reads, writes, tag=tag)

    def act(self, out, in_, func, reads, writes, bias=0.0, scale=1.0, tag="act"):
        reads = list(reads)
        reads.append(Prog.R(self.cst_sb))
        return self.p.emit(
            "act",
            lambda e, o=out, i=in_, f=func, b=bias, s=scale:
                e.activation(out=o, in_=i, func=f, bias=b, scale=s),
            reads, writes, tag=tag)

    def acta(self, out, in_, addv, reads, writes, tag="acta"):
        reads = list(reads) + [Prog.R(self.cst_sb)]
        return self.p.emit(
            "act", lambda e, o=out, i=in_, a=addv: e.add(out=o, in_=i, add=a),
            reads, writes, tag=tag)

    def copy(self, out, in_, reads, writes, eng="act", tag="cp"):
        if eng == "act":
            reads = list(reads) + [Prog.R(self.cst_sb)]
            return self.p.emit("act", lambda e, o=out, i=in_: e.copy(out=o, in_=i),
                               reads, writes, tag=tag)
        return self.p.emit(eng, lambda e, o=out, i=in_: e.tensor_copy(out=o, in_=i),
                           reads, writes, tag=tag)

    def tt(self, out, a, b, op, reads, writes, eng="dve", tag="tt"):
        return self.p.emit(
            eng,
            lambda e, o=out, x=a, y=b, f=op: e.tensor_tensor(out=o, in0=x, in1=y, op=f),
            reads, writes, tag=tag)

    def ts(self, out, a, s1, op0, reads, writes, s2=None, op1=None, eng="dve",
           tag="ts"):
        def f(e, o=out, x=a, u=s1, v=s2, f0=op0, f1=op1):
            if v is None:
                return e.tensor_single_scalar(out=o, in_=x, scalar=u, op=f0)
            return e.tensor_scalar(out=o, in0=x, scalar1=u, scalar2=v, op0=f0, op1=f1)
        return self.p.emit(eng, f, reads, writes, tag=tag)

    def stt(self, out, in0, scalar, in1, op0, op1, reads, writes, eng="dve",
            tag="stt"):
        return self.p.emit(
            eng,
            lambda e, o=out, x=in0, s=scalar, y=in1, f0=op0, f1=op1:
                e.scalar_tensor_tensor(out=o, in0=x, scalar=s, in1=y, op0=f0, op1=f1),
            reads, writes, tag=tag)

    def scan(self, out, d0, d1, init, reads, writes, tag="scan"):
        return self.p.emit(
            "dve",
            lambda e, o=out, a=d0, b=d1, i=init:
                e.tensor_tensor_scan(out=o, data0=a, data1=b, initial=i,
                                     op0=OP.mult, op1=OP.add),
            reads, writes, tag=tag)

    def red(self, out, in_, op, reads, writes, tag="red"):
        return self.p.emit(
            "dve",
            lambda e, o=out, i=in_, f=op:
                e.tensor_reduce(out=o, in_=i, axis=AX.X, op=f),
            reads, writes, tag=tag)

    def recip(self, out, in_, reads, writes, tag="rcp"):
        return self.p.emit(
            "dve", lambda e, o=out, i=in_: e.reciprocal(out=o, in_=i),
            reads, writes, tag=tag)

    def memset(self, ap, val, reads, writes, eng="dve", tag="ms"):
        return self.p.emit(eng, lambda e, o=ap, v=val: e.memset(o, v),
                           reads, writes, tag=tag)

    def dbg_dump(self, name, src_handle, shape, rr):
        if not self.dbg:
            return
        t = src_handle
        d = self.nc.dram_tensor(name, list(t.shape), t.dtype, kind="ExternalOutput")
        self.dma(d.ap(), t.ap(), [rr], [Prog.R(d)], tag="dbg")

    def cc(self, kind, op, groups, ins, outs, reads, writes, tag="cc"):
        return self.p.emit(
            "pool",
            lambda e, k=kind, o=op, g=groups, i=tuple(ins), u=tuple(outs):
                e.collective_compute(k, o, replica_groups=g, ins=list(i),
                                     outs=list(u)),
            reads, writes, step=16, tag=tag)

    def build(self):
        cf = self.cfg
        C, CK, T, TC, NTC = cf.C, cf.CK, cf.T, cf.TC, cf.NTC
        DMY, DMK, NST, DTR = cf.DMY, cf.DMK, cf.NST, cf.DTR
        R = Prog.R

        di = lambda n, sh: self.dram(n, sh, F32, "ExternalInput")
        x_full = di("x_full", [T, C])
        identv = di("identv", [C])
        v1c = di("v1c", [C, 1]); v2c = di("v2c", [C, 1])
        pmat = {1: (di("p1r", [C, C]), di("p1c", [C, C]), v1c),
                2: (di("p2r", [C, C]), di("p2c", [C, C]), v2c)}
        wqT = di("wqT", [C, C]); wvT = di("wvT", [C, C]); wcT = di("wcT", [C, C])
        ffn_wT = di("ffn_wT", [C, C]); ffn_b = di("ffn_b", [C])
        ffn_da = di("ffn_da", [100, C]); ffn_ldT = di("ffn_ldT", [100, 4])
        ffn_luT = di("ffn_luT", [4, 50]); ffn_uaT = di("ffn_uaT", [50, C])
        onehot = di("onehot", [48, NST * 128])
        ident128 = di("ident128", [128, 128])
        winxT = di("winxT", [C, DMY]); winzT = di("winzT", [C, DMY])
        woutT = di("woutT", [DMY, C])
        mbw = {}
        for p_ in ("f", "b"):
            mbw[p_] = dict(
                cw=di(f"cw_{p_}", [DMY, 4]), cb=di(f"cb_{p_}", [DMY]),
                wxT=di(f"wxT_{p_}", [DMY, 96]), wdtT=di(f"wdtT_{p_}", [DTR, DMY]),
                bdt=di(f"bdt_{p_}", [DMY]), alog=di(f"alog_{p_}", [DMY, NST]),
                dv=di(f"dv_{p_}", [DMY]))
        out_d = self.dram("out", [T, C], F32, "ExternalOutput")

        xres = self.dram("xres", [CK, 128, T], F32)
        TE2 = T // 2 + 1
        l0e = self.dram("l0e", [CK, 128, TE2], BF16)
        l0o = self.dram("l0o", [CK, 128, T // 2], BF16)
        u_sc = {p_: self.dram(f"u_{p_}", [DMK, 128, T], BF16) for p_ in ("f", "b")}
        sz_sc = self.dram("szs", [DMK, 128, T], BF16)
        outp = self.dram("outp", [CK, 128, T], BF16)
        xdb_d = self.dram("xdb_d", [2, 96, T], BF16)

        self.mk_psum(8)

        cstv = self.dram("cstv", [128, 8], F32, "ExternalInput")
        cst_sb = self.sb("cst_sb", [128, 8], F32)
        self.cst_sb = cst_sb
        self.dma(cst_sb[:], cstv.ap(), [R(cstv)], [R(cst_sb)], tag="ldc")
        for i_, v_ in enumerate(CONSTS):
            self.nc.const_aps.register(F32, v_, cst_sb[:, i_:i_ + 1])

        id128 = self.sb("id128", [128, 128], F32)
        ones_c_bf = self.sb("ones_c_bf", [128, 1], BF16)
        ones_r_f = self.sb("ones_r_f", [1, 128], F32)
        self.dma(id128[:], ident128.ap(), [R(ident128)], [R(id128)], tag="ldc")
        self.memset(ones_c_bf[:], 1.0, [], [R(ones_c_bf)])
        self.memset(ones_r_f[:], 1.0, [], [R(ones_r_f)])

        xc = self.sb("xc", [128, CK * TC], F32)
        xbf = self.sb("xbf", [128, CK * TC], BF16)
        sqbf = self.sb("sqbf", [128, CK * TC], BF16)
        xn = self.sb("xn", [128, CK * TC], BF16)
        srow = self.sb("srow", [1, 4 * TC], F32)

        xres_r = xres.ap().rearrange("c p t -> p c t")

        def layernorm_chunk(ph):
            invC = 1.0 / C
            self.copy(xbf[:], xc[:], [R(xc)], [R(xbf)], tag=f"{ph}lncp")
            self.tt(sqbf[:], xbf[:], xbf[:], OP.mult, [R(xbf)], [R(sqbf)],
                    tag=f"{ph}lnsq")
            ps_s = self.nps()
            ps_q = self.nps()
            self.mmacc(ps_s[0:1, 0:TC],
                       [(ones_c_bf[:, 0:1], xbf[:, k * TC:(k + 1) * TC])
                        for k in range(CK)],
                       [R(ones_c_bf), R(xbf)], [R(ps_s, 0, TC)], tag=f"{ph}lnS")
            self.mmacc(ps_q[0:1, 0:TC],
                       [(ones_c_bf[:, 0:1], sqbf[:, k * TC:(k + 1) * TC])
                        for k in range(CK)],
                       [R(ones_c_bf), R(sqbf)], [R(ps_q, 0, TC)], tag=f"{ph}lnQ")
            m_r = srow[0:1, 0:TC]; q_r = srow[0:1, TC:2 * TC]
            v_r = srow[0:1, 2 * TC:3 * TC]; r_r = srow[0:1, 3 * TC:4 * TC]
            self.ts(m_r, ps_s[0:1, 0:TC], invC, OP.mult,
                    [R(ps_s, 0, TC)], [R(srow, 0, TC)], tag=f"{ph}lnm")
            self.ts(q_r, ps_q[0:1, 0:TC], invC, OP.mult,
                    [R(ps_q, 0, TC)], [R(srow, TC, 2 * TC)], tag=f"{ph}lnq")
            self.tt(v_r, m_r, m_r, OP.mult, [R(srow, 0, TC)],
                    [R(srow, 2 * TC, 3 * TC)], tag=f"{ph}lnv0")
            self.tt(v_r, q_r, v_r, OP.subtract,
                    [R(srow, TC, 3 * TC)], [R(srow, 2 * TC, 3 * TC)], tag=f"{ph}lnv1")
            self.act(v_r, v_r, AF.Ln, [R(srow, 2 * TC, 3 * TC)],
                     [R(srow, 2 * TC, 3 * TC)], bias=float(cf.EPS), tag=f"{ph}lnln")
            self.act(r_r, v_r, AF.Exp, [R(srow, 2 * TC, 3 * TC)],
                     [R(srow, 3 * TC, 4 * TC)], scale=-0.5, tag=f"{ph}lnex")
            ps_m = self.nps()
            ps_r = self.nps()
            self.mmacc(ps_m[0:128, 0:TC], [(ones_r_f[0:1, :], m_r)],
                       [R(ones_r_f), R(srow, 0, TC)], [R(ps_m, 0, TC)],
                       tag=f"{ph}lnbm")
            self.mmacc(ps_r[0:128, 0:TC], [(ones_r_f[0:1, :], r_r)],
                       [R(ones_r_f), R(srow, 3 * TC, 4 * TC)], [R(ps_r, 0, TC)],
                       tag=f"{ph}lnbr")
            for c in range(CK):
                sl = slice(c * TC, (c + 1) * TC)
                self.tt(xbf[:, sl], xc[:, sl], ps_m[0:128, 0:TC], OP.subtract,
                        [R(xc, c * TC, (c + 1) * TC), R(ps_m, 0, TC)],
                        [R(xbf, c * TC, (c + 1) * TC)], tag=f"{ph}lnsub")
                self.tt(xn[:, sl], xbf[:, sl], ps_r[0:128, 0:TC], OP.mult,
                        [R(xbf, c * TC, (c + 1) * TC), R(ps_r, 0, TC)],
                        [R(xn, c * TC, (c + 1) * TC)], tag=f"{ph}lnmul")

        rn_sets = []
        for i_ in range(2):
            rn_sets.append(dict(
                zmin=self.sb(f"rnP_min{i_}", [128, 512], F32),
                zmax=self.sb(f"rnP_max{i_}", [128, 512], F32),
                mt=self.sb(f"rnP_mt{i_}", [128, 8], F32),
                sc=self.sb(f"rnP_sc{i_}", [128, 8], F32),
                rows=self.sb(f"rnP_rw{i_}", [1, 1024], F32),
                t=self.sb(f"rnP_t{i_}", [128, 512], F32),
            ))

        def rangenorm(zf, NS, apply_fns, ph):
            st_ = rn_sets[zfi[0] % 2]
            NB = (NS + 127) // 128
            w1 = min(128, NS)
            zmin = st_["zmin"]
            zmax = st_["zmax"]
            self.copy(zmin[:, 0:NS], zf[:, 0:NS], [R(zf, 0, NS)], [R(zmin)],
                      eng="dve", tag=f"{ph}rn0")
            self.copy(zmax[:, 0:NS], zf[:, 0:NS], [R(zf, 0, NS)], [R(zmax)],
                      eng="dve", tag=f"{ph}rn1")
            for c in range(1, CK):
                self.tt(zmin[:, 0:NS], zmin[:, 0:NS], zf[:, c * NS:(c + 1) * NS],
                        OP.min,
                        [R(zmin), R(zf, c * NS, (c + 1) * NS)], [R(zmin)],
                        tag=f"{ph}rn2")
                self.tt(zmax[:, 0:NS], zmax[:, 0:NS], zf[:, c * NS:(c + 1) * NS],
                        OP.max,
                        [R(zmax), R(zf, c * NS, (c + 1) * NS)], [R(zmax)],
                        tag=f"{ph}rn3")
            mt = st_["mt"]
            if NS % 128:
                self.memset(mt[:], 0.0, [], [R(mt)], tag=f"{ph}rnz")
            for off, (src, rop) in enumerate(((zmin, OP.min), (zmax, OP.max))):
                for b_ in range(NB):
                    pt = self.nps()
                    ww = min(128, NS - b_ * 128)
                    self.tr(pt[0:ww, 0:128], src[:, b_ * 128:b_ * 128 + ww],
                            id128[:],
                            [R(src, b_ * 128, b_ * 128 + ww), R(id128)],
                            [R(pt, 0, 128)], tag=f"{ph}rn4")
                    self.red(mt[0:ww, off * NB + b_:off * NB + b_ + 1],
                             pt[0:ww, 0:128], rop,
                             [R(pt, 0, 128)],
                             [R(mt, off * NB + b_, off * NB + b_ + 1)],
                             tag=f"{ph}rn5")
            sc_t = st_["sc"]
            self.tt(sc_t[0:w1, 0:NB], mt[0:w1, NB:2 * NB], mt[0:w1, 0:NB],
                    OP.subtract, [R(mt)], [R(sc_t, 0, NB)], tag=f"{ph}rn6")
            self.ts(sc_t[0:w1, 0:NB], sc_t[0:w1, 0:NB], 1e-8, OP.add,
                    [R(sc_t, 0, NB)], [R(sc_t, 0, NB)], tag=f"{ph}rn7")
            self.recip(sc_t[0:w1, 0:NB], sc_t[0:w1, 0:NB],
                       [R(sc_t, 0, NB)], [R(sc_t, 0, NB)], tag=f"{ph}rn8")
            self.ts(sc_t[0:w1, 0:NB], sc_t[0:w1, 0:NB], 2.0, OP.mult,
                    [R(sc_t, 0, NB)], [R(sc_t, 0, NB)], tag=f"{ph}rn9")
            self.tt(sc_t[0:w1, NB:2 * NB], mt[0:w1, 0:NB], sc_t[0:w1, 0:NB],
                    OP.mult, [R(mt, 0, NB), R(sc_t, 0, NB)],
                    [R(sc_t, NB, 2 * NB)], tag=f"{ph}rnA")
            self.ts(sc_t[0:w1, NB:2 * NB], sc_t[0:w1, NB:2 * NB], -1.0, OP.mult,
                    [R(sc_t, NB, 2 * NB)], [R(sc_t, NB, 2 * NB)],
                    s2=-1.0, op1=OP.add, tag=f"{ph}rnB")
            rows = st_["rows"]
            for j, off in ((0, 0), (1, NS)):
                for b_ in range(NB):
                    pt = self.nps()
                    ww = min(128, NS - b_ * 128)
                    self.tr(pt[0:1, 0:ww],
                            sc_t[0:ww, j * NB + b_:j * NB + b_ + 1],
                            id128[0:ww, 0:ww],
                            [R(sc_t, j * NB + b_, j * NB + b_ + 1), R(id128)],
                            [R(pt, 0, ww)], tag=f"{ph}rnC")
                    self.copy(rows[0:1, off + b_ * 128:off + b_ * 128 + ww],
                              pt[0:1, 0:ww], [R(pt, 0, ww)],
                              [R(rows, off + b_ * 128, off + b_ * 128 + ww)],
                              tag=f"{ph}rnD")
            ps_sc = self.nps()
            ps_sh = self.nps()
            self.mmacc(ps_sc[0:128, 0:NS], [(ones_r_f[0:1, :], rows[0:1, 0:NS])],
                       [R(ones_r_f), R(rows, 0, NS)], [R(ps_sc, 0, NS)],
                       tag=f"{ph}rnE")
            self.mmacc(ps_sh[0:128, 0:NS],
                       [(ones_r_f[0:1, :], rows[0:1, NS:2 * NS])],
                       [R(ones_r_f), R(rows, NS, 2 * NS)], [R(ps_sh, 0, NS)],
                       tag=f"{ph}rnF")
            apply_fns(ps_sc, ps_sh)

        xtok = [self.sb(f"xtok{i}", [128, C], F32) for i in range(2)]
        xfm = [self.sb(f"xfm{i}", [128, CK * 128], F32) for i in range(2)]
        for tt_ in range(T // 128):
            t0 = tt_ * 128
            pp = tt_ % 2
            self.dma(xtok[pp][:], x_full[t0:t0 + 128, :], [R(x_full)],
                     [R(xtok[pp])], q=("sp" if pp == 0 else "act"), tag="pinL")
            for c in range(CK):
                pt = self.nps()
                self.tr(pt[0:128, 0:128], xtok[pp][:, c * 128:(c + 1) * 128],
                        id128[:],
                        [R(xtok[pp], c * 128, (c + 1) * 128), R(id128)],
                        [R(pt, 0, 128)], tag="pinT")
                self.copy(xfm[pp][:, c * 128:(c + 1) * 128], pt[0:128, 0:128],
                          [R(pt, 0, 128)],
                          [R(xfm[pp], c * 128, (c + 1) * 128)], tag="pinC")
            self.dma(xres_r[:, :, t0:t0 + 128], xfm[pp][:],
                     [R(xfm[pp])], [R(xres, t0, t0 + 128)],
                     q="pool", tag="pinS")
        self.free(xtok[0], xtok[1], xfm[0], xfm[1])

        owc = {}
        for mi in (1, 2):
            pr_d, pc_d, vc_d = pmat[mi]
            ur = self.sb(f"po_ur{mi}", [128, CK * C], F32)
            uc = self.sb(f"po_uc{mi}", [128, CK * C], F32)
            vcol = self.sb(f"po_v{mi}", [128, CK], F32)
            scl = self.sb(f"po_s{mi}", [128, 2], F32)
            ssc = self.sb(f"po_ss{mi}", [1, 8], F32)
            ycol = self.sb(f"po_y{mi}", [128, CK], F32)
            self.dma(ur[:], pr_d.ap().rearrange("(rc p) c2 -> p rc c2", p=128),
                     [R(pr_d)], [R(ur)], tag="poL")
            self.dma(uc[:], pc_d.ap().rearrange("(cc p) r2 -> p cc r2", p=128),
                     [R(pc_d)], [R(uc)], tag="poL")
            self.dma(vcol[:], vc_d.ap().rearrange("(c p) one -> p c one", p=128),
                     [R(vc_d)], [R(vcol)], tag="poL", slow=True)
            for it in range(2):
                for mc in range(CK):
                    pt = self.nps()
                    self.mmacc(pt[0:128, 0:1],
                               [(uc[:, kc * C + mc * 128:kc * C + (mc + 1) * 128],
                                 vcol[:, kc:kc + 1]) for kc in range(CK)],
                               [R(uc), R(vcol)], [R(pt, 0, 1)], tag="poY")
                    self.copy(ycol[:, mc:mc + 1], pt[0:128, 0:1], [R(pt, 0, 1)],
                              [R(ycol, mc, mc + 1)], tag="poYc")
                pss = self.nps()
                self.mmacc(pss[0:1, 0:1],
                           [(ycol[:, mc:mc + 1], ycol[:, mc:mc + 1])
                            for mc in range(CK)],
                           [R(ycol)], [R(pss, 0, 1)], tag="poSS")
                self.act(ssc[0:1, 0:1], pss[0:1, 0:1], AF.Sqrt,
                         [R(pss, 0, 1)], [R(ssc, 0, 1)], tag="poSq")
                self.ts(ssc[0:1, 1:2], ssc[0:1, 0:1], 1e-8, OP.add,
                        [R(ssc, 0, 1)], [R(ssc, 1, 2)], tag="poS1")
                self.recip(ssc[0:1, 2:3], ssc[0:1, 1:2], [R(ssc, 1, 2)],
                           [R(ssc, 2, 3)], tag="poS2")
                self.tt(ssc[0:1, 3:4], ssc[0:1, 2:3], ssc[0:1, 2:3], OP.mult,
                        [R(ssc, 2, 3)], [R(ssc, 3, 4)], tag="poS3")
                psb = self.nps()
                self.mmacc(psb[0:128, 0:2],
                           [(ones_r_f[0:1, :], ssc[0:1, 2:4])],
                           [R(ones_r_f), R(ssc, 2, 4)], [R(psb, 0, 2)], tag="poS5")
                self.copy(scl[:, 0:2], psb[0:128, 0:2], [R(psb, 0, 2)],
                          [R(scl, 0, 2)], tag="poS6")
                self.ts(scl[:, 1:2], scl[:, 1:2], 0.5, OP.mult,
                        [R(scl, 1, 2)], [R(scl, 1, 2)], tag="poS7")
                last = it == 1
                nt = self.sb(f"po_n{mi}{it}", [128, CK * C], F32)
                self.ts(nt[:], ur[:], scl[:, 1:2], OP.mult,
                        [R(ur), R(scl, 1, 2)], [R(nt)], tag="poU0")
                self.stt(nt[:], uc[:], 0.5, nt[:], OP.mult, OP.subtract,
                         [R(uc), R(nt)], [R(nt)], tag="poU1")
                for dk in range(CK):
                    d0 = dk * C + dk * 128
                    self.stt(nt[:, d0:d0 + 128], id128[:], scl[:, 0:1],
                             nt[:, d0:d0 + 128], OP.mult, OP.add,
                             [R(id128), R(scl, 0, 1), R(nt, d0, d0 + 128)],
                             [R(nt, d0, d0 + 128)], tag="poU2")
                self.free(ur, uc)
                uc = nt
                if not last:
                    ur = self.sb(f"po_r{mi}{it}", [128, CK * C], F32)
                    for rc in range(CK):
                        for cc in range(CK):
                            pt = self.nps()
                            src0 = cc * C + rc * 128
                            dst0 = rc * C + cc * 128
                            self.tr(pt[0:128, 0:128], uc[:, src0:src0 + 128],
                                    id128[:],
                                    [R(uc, src0, src0 + 128), R(id128)],
                                    [R(pt, 0, 128)], tag="poT")
                            self.copy(ur[:, dst0:dst0 + 128], pt[0:128, 0:128],
                                      [R(pt, 0, 128)],
                                      [R(ur, dst0, dst0 + 128)], tag="poTc")
            ow = self.sb(f"ow{mi}c", [128, CK * C], BF16)
            self.copy(ow[:], uc[:], [R(uc)], [R(ow)], tag="poB")
            if self.dbg:
                dd = self.nc.dram_tensor(f"dbg_ow{mi}", [128, CK * C], BF16,
                                         kind="ExternalOutput")
                self.dma(dd.ap(), ow[:], [R(ow)], [Prog.R(dd)], tag="dbg")
            self.free(uc, vcol, scl, ssc, ycol)
            owc[mi] = ow

        wtmp = self.sb("wtmp", [128, CK * C], F32)
        weff = self.sb("weff", [128, CK * C], BF16)
        fb_col = self.sb("fb_col", [128, CK], F32)
        da_s = self.sb("da_s", [100, C], F32)
        ldT_s = self.sb("ldT_s", [100, 4], F32)
        luT_s = self.sb("luT_s", [4, 50], F32)
        uaT_s = self.sb("uaT_s", [50, C], F32)
        t1_s = self.sb("t1_s", [128, CK * 4], F32)
        t1T_s = self.sb("t1T_s", [4, C], F32)
        t2T_s = self.sb("t2T_s", [50, C], F32)
        self.dma(da_s[:], ffn_da.ap(), [R(ffn_da)], [R(da_s)], tag="fwL")
        self.dma(ldT_s[:], ffn_ldT.ap(), [R(ffn_ldT)], [R(ldT_s)], tag="fwL")
        self.dma(luT_s[:], ffn_luT.ap(), [R(ffn_luT)], [R(luT_s)], tag="fwL")
        self.dma(uaT_s[:], ffn_uaT.ap(), [R(ffn_uaT)], [R(uaT_s)], tag="fwL")
        self.dma(wtmp[:], ffn_wT.ap().rearrange("(kc p) m -> p kc m", p=128),
                 [R(ffn_wT)], [R(wtmp)], tag="fwL")
        self.dma(fb_col[:], ffn_b.ap().rearrange("(c p) -> p c", p=128),
                 [R(ffn_b)], [R(fb_col)], tag="fwL", slow=True)
        for mc in range(CK):
            pt = self.nps()
            self.mmacc(pt[0:128, 0:4],
                       [(da_s[0:100, mc * 128:(mc + 1) * 128], ldT_s[0:100, 0:4])],
                       [R(da_s), R(ldT_s)], [R(pt, 0, 4)], tag="fw1")
            self.copy(t1_s[:, mc * 4:(mc + 1) * 4], pt[0:128, 0:4],
                      [R(pt, 0, 4)], [R(t1_s, mc * 4, (mc + 1) * 4)], tag="fw1c")
        for mc in range(CK):
            pt = self.nps()
            self.tr(pt[0:4, 0:128], t1_s[:, mc * 4:(mc + 1) * 4], id128[:],
                    [R(t1_s, mc * 4, (mc + 1) * 4), R(id128)], [R(pt, 0, 128)],
                    tag="fw2")
            self.copy(t1T_s[0:4, mc * 128:(mc + 1) * 128], pt[0:4, 0:128],
                      [R(pt, 0, 128)], [R(t1T_s, mc * 128, (mc + 1) * 128)],
                      tag="fw2c")
        for half in range((C + 511) // 512):
            n0, n1 = half * 512, min(C, (half + 1) * 512)
            pt = self.nps()
            self.mmacc(pt[0:50, 0:n1 - n0],
                       [(luT_s[0:4, 0:50], t1T_s[0:4, n0:n1])],
                       [R(luT_s), R(t1T_s)], [R(pt, 0, n1 - n0)], tag="fw3")
            self.copy(t2T_s[0:50, n0:n1], pt[0:50, 0:n1 - n0],
                      [R(pt, 0, n1 - n0)], [R(t2T_s, n0, n1)], tag="fw3c")
        for mc in range(CK):
            for half in range((C + 511) // 512):
                n0, n1 = half * 512, min(C, (half + 1) * 512)
                pt = self.nps()
                self.mmacc(pt[0:128, 0:n1 - n0],
                           [(t2T_s[0:50, mc * 128:(mc + 1) * 128],
                             uaT_s[0:50, n0:n1])],
                           [R(t2T_s), R(uaT_s)], [R(pt, 0, n1 - n0)], tag="fw4")
                self.tt(wtmp[:, mc * C + n0:mc * C + n1],
                        wtmp[:, mc * C + n0:mc * C + n1], pt[0:128, 0:n1 - n0],
                        OP.add,
                        [R(wtmp, mc * C + n0, mc * C + n1), R(pt, 0, n1 - n0)],
                        [R(wtmp, mc * C + n0, mc * C + n1)], tag="fw4a")
        self.copy(weff[:], wtmp[:], [R(wtmp)], [R(weff)], tag="fwB")
        weffs_d = self.dram("weffs", [128, CK * C], BF16)
        self.dma(weffs_d.ap(), weff[:], [R(weff)], [R(weffs_d)], tag="fwP")
        self.free(wtmp, da_s, ldT_s, luT_s, uaT_s, t1_s, t1T_s, t2T_s, weff)

        wq_bf = self.sb("wq_bf", [128, CK * C], BF16)
        self.dma(wq_bf[:], wqT.ap().rearrange("(kc p) m -> p kc m", p=128),
                 [R(wqT)], [R(wq_bf)], q="pool", tag="p2L")
        idf = self.sb("idf", [128, CK], F32)
        idc = self.sb("idc", [128, CK], BF16)
        self.dma(idf[:], identv.ap().rearrange("(c p) -> p c", p=128),
                 [R(identv)], [R(idf)], tag="p2L", slow=True)
        self.copy(idc[:], idf[:], [R(idf)], [R(idc)], tag="p2B")
        self.dma(l0e.ap().rearrange("c p t -> p c t")[:, :, 0:1], idc[:],
                 [R(idc)], [R(l0e, 0, 1)], tag="p2S", slow=True)
        qbf = self.sb("qbf", [128, CK * TC], BF16)
        H = TC // 2
        qsp = self.sb("qsp", [128, CK * H], BF16)
        for tci in range(NTC):
            t0 = tci * TC
            self.dma(xc[:], xres_r[:, :, t0:t0 + TC],
                     [R(xres, t0, t0 + TC)], [R(xc)], tag="p2X")
            layernorm_chunk("p2")
            for mc in range(CK):
                pt = self.nps()
                self.mmacc(pt[0:128, 0:TC],
                           [(wq_bf[:, kc * C + mc * 128:kc * C + (mc + 1) * 128],
                             xn[:, kc * TC:(kc + 1) * TC]) for kc in range(CK)],
                           [R(wq_bf), R(xn)], [R(pt, 0, TC)], tag="p2M")
                self.copy(qbf[:, mc * TC:(mc + 1) * TC], pt[0:128, 0:TC],
                          [R(pt, 0, TC)], [R(qbf, mc * TC, (mc + 1) * TC)],
                          tag="p2E")
            for par, dst, off in ((0, l0o, t0 // 2), (1, l0e, t0 // 2 + 1)):
                for c in range(CK):
                    self.copy(qsp[:, c * H:(c + 1) * H],
                              qbf[:, c * TC + par:(c + 1) * TC:2],
                              [R(qbf, c * TC, (c + 1) * TC)],
                              [R(qsp, c * H, (c + 1) * H)], eng="dve", tag="p2sp")
                self.dma(dst.ap().rearrange("c p t -> p c t")[:, :, off:off + H],
                         qsp[:], [R(qsp)], [R(dst, off, off + H)], tag="p2S")
        self.free(wq_bf, idc, idf, qbf, qsp)
        self.free(xc, xbf, sqbf, xn, srow)

        m = [T + 1]
        while m[-1] > 1:
            m.append(m[-1] // 2)
        NLEV = len(m) - 1
        Le = {}; Lo = {}
        for j in range(1, NLEV + 1):
            ne = (m[j] + 1) // 2
            no = m[j] // 2
            Le[j] = self.sb(f"Le{j}", [128, CK * max(ne, 1)], BF16)
            if no:
                Lo[j] = self.sb(f"Lo{j}", [128, CK * no], BF16)
        zfs = [self.sb(f"zf{i}", [128, CK * 512], F32) for i in range(2)]
        zfi = [0]
        lea = self.sb("lea", [128, CK * 512], BF16)
        loa = self.sb("loa", [128, CK * 512], BF16)

        def combine_block(le_f, lo_f, nsub, reads, tag):
            zf = zfs[zfi[0] % 2]
            for mc in range(CK):
                pt = self.nps()
                pieces = []
                for kc in range(CK):
                    pieces.append(
                        (owc[1][:, kc * C + mc * 128:kc * C + (mc + 1) * 128],
                         le_f(kc)))
                for kc in range(CK):
                    pieces.append(
                        (owc[2][:, kc * C + mc * 128:kc * C + (mc + 1) * 128],
                         lo_f(kc)))
                self.mmacc(pt[0:128, 0:nsub], pieces,
                           reads + [R(owc[1]), R(owc[2])], [R(pt, 0, nsub)],
                           tag=tag + "m")
                self.copy(zf[:, mc * nsub:(mc + 1) * nsub], pt[0:128, 0:nsub],
                          [R(pt, 0, nsub)], [R(zf, mc * nsub, (mc + 1) * nsub)],
                          tag=tag + "e")

        l0done = [False]
        for j in range(NLEV):
            if j == 1 and not l0done[0]:
                self.free(lea, loa)
                l0done[0] = True
            nout = m[j + 1]
            NS = min(512, max(nout, 1))
            for s_ in range((nout + NS - 1) // NS):
                o0 = s_ * NS
                w = min(NS, nout - o0)
                if j == 0:
                    self.dma(lea[:, 0:CK * w].rearrange("p (c t) -> p c t", c=CK),
                             l0e.ap().rearrange("c p t -> p c t")[:, :, o0:o0 + w],
                             [R(l0e, o0, o0 + w)], [R(lea, 0, CK * w)], tag="d0Le")
                    self.dma(loa[:, 0:CK * w].rearrange("p (c t) -> p c t", c=CK),
                             l0o.ap().rearrange("c p t -> p c t")[:, :, o0:o0 + w],
                             [R(l0o, o0, o0 + w)], [R(loa, 0, CK * w)], tag="d0Lo")
                    le_f = lambda kc, w=w: lea[:, kc * w:kc * w + w]
                    lo_f = lambda kc, w=w: loa[:, kc * w:kc * w + w]
                    rd_ = [R(lea, 0, CK * w), R(loa, 0, CK * w)]
                else:
                    ne_j = (m[j] + 1) // 2
                    no_j = m[j] // 2
                    le_f = (lambda kc, o0=o0, w=w, ne_j=ne_j:
                            Le[j][:, kc * ne_j + o0:kc * ne_j + o0 + w])
                    lo_f = (lambda kc, o0=o0, w=w, no_j=no_j:
                            Lo[j][:, kc * no_j + o0:kc * no_j + o0 + w])
                    rd_ = [R(Le[j]), R(Lo[j])]
                zfi[0] += 1
                combine_block(le_f, lo_f, w, rd_, f"d{j}")
                zf = zfs[zfi[0] % 2]
                ne1 = (m[j + 1] + 1) // 2
                no1 = m[j + 1] // 2

                def apply_rn(ps_sc, ps_sh, j=j, o0=o0, w=w, ne1=ne1, no1=no1,
                             zf=zf, t_=rn_sets[zfi[0] % 2]["t"]):
                    for c in range(CK):
                        for par in (0, 1):
                            cnt = (w - par + 1) // 2
                            if cnt <= 0:
                                continue
                            g0 = o0 + par
                            di_ = g0 // 2
                            dst = Lo[j + 1] if (g0 % 2) else Le[j + 1]
                            wdt = no1 if (g0 % 2) else ne1
                            self.tt(t_[:, 0:cnt], zf[:, c * w + par:c * w + w:2],
                                    ps_sc[0:128, par:w:2], OP.mult,
                                    [R(zf, c * w, (c + 1) * w), R(ps_sc, 0, w)],
                                    [R(t_)], tag=f"d{j}a0")
                            self.tt(dst[:, c * wdt + di_:c * wdt + di_ + cnt],
                                    t_[:, 0:cnt], ps_sh[0:128, par:w:2], OP.add,
                                    [R(t_), R(ps_sh, 0, w)],
                                    [R(dst, c * wdt + di_, c * wdt + di_ + cnt)],
                                    tag=f"d{j}a1")
                rangenorm(zf, w, apply_rn, f"d{j}")

        if self.dbg:
            for j in range(1, NLEV + 1):
                ne = (m[j] + 1) // 2
                dd = self.nc.dram_tensor(f"dbg_Le{j}", [128, CK * max(ne, 1)],
                                         BF16, kind="ExternalOutput")
                self.dma(dd.ap(), Le[j][:], [R(Le[j])], [Prog.R(dd)], tag="dbg")
                if j in Lo:
                    no = m[j] // 2
                    dd2 = self.nc.dram_tensor(f"dbg_Lo{j}", [128, CK * no],
                                              BF16, kind="ExternalOutput")
                    self.dma(dd2.ap(), Lo[j][:], [R(Lo[j])], [Prog.R(dd2)],
                             tag="dbg")

        Sv = {NLEV: Le[NLEV]}
        for j in range(NLEV - 1, 0, -1):
            mj = m[j]
            Sj = self.sb(f"S{j}", [128, CK * mj], BF16)
            Sv[j] = Sj
            ne_j = (mj + 1) // 2
            prev = Sv[j + 1]
            mprev = m[j + 1]
            for c in range(CK):
                self.copy(Sj[:, c * mj + 1:c * mj + mj:2],
                          prev[:, c * mprev:(c + 1) * mprev],
                          [R(prev)], [R(Sj, c * mj, (c + 1) * mj)], eng="dve",
                          tag=f"u{j}o")
                self.copy(Sj[:, c * mj:c * mj + 1],
                          Le[j][:, c * ne_j:c * ne_j + 1],
                          [R(Le[j])], [R(Sj, c * mj, c * mj + 1)], eng="dve",
                          tag=f"u{j}z")
            nev = mj // 2 - 1
            if nev > 0:
                NS = min(512, nev)
                for s_ in range((nev + NS - 1) // NS):
                    o0 = s_ * NS
                    w = min(NS, nev - o0)
                    le_f = (lambda kc, o0=o0, w=w, mprev=mprev:
                            prev[:, kc * mprev + o0:kc * mprev + o0 + w])
                    lo_f = (lambda kc, o0=o0, w=w, ne_j=ne_j:
                            Le[j][:, kc * ne_j + o0 + 1:kc * ne_j + o0 + 1 + w])
                    zfi[0] += 1
                    combine_block(le_f, lo_f, w, [R(prev), R(Le[j])], f"u{j}")
                    zf = zfs[zfi[0] % 2]

                    def apply_rn(ps_sc, ps_sh, j=j, o0=o0, w=w, mj=mj, Sj=Sj,
                                 zf=zf, t_=rn_sets[zfi[0] % 2]["t"]):
                        for c in range(CK):
                            self.tt(t_[:, 0:w], zf[:, c * w:(c + 1) * w],
                                    ps_sc[0:128, 0:w], OP.mult,
                                    [R(zf, c * w, (c + 1) * w), R(ps_sc, 0, w)],
                                    [R(t_)], tag=f"u{j}a0")
                            d0 = c * mj + 2 * (o0 + 1)
                            self.tt(Sj[:, d0:d0 + 2 * w:2], t_[:, 0:w],
                                    ps_sh[0:128, 0:w], OP.add,
                                    [R(t_), R(ps_sh, 0, w)],
                                    [R(Sj, c * mj, (c + 1) * mj)], tag=f"u{j}a1")
                    rangenorm(zf, w, apply_rn, f"u{j}")
            if self.dbg:
                dd = self.nc.dram_tensor(f"dbg_S{j}", [128, CK * mj], BF16,
                                         kind="ExternalOutput")
                self.dma(dd.ap(), Sj[:], [R(Sj)], [Prog.R(dd)], tag="dbg")
            if (j + 1) in Lo:
                self.free(Lo[j + 1])
            if Sv[j + 1] is Le[j + 1]:
                self.free(Le[j + 1])
            else:
                self.free(Sv[j + 1], Le[j + 1])
        S1 = Sv[1]
        m1 = m[1]
        self.free(Lo[1], Le[1])
        if self.dbg:
            self.dbg_dump("dbg_l0e", l0e, None, Prog.R(l0e))
            self.dbg_dump("dbg_l0o", l0o, None, Prog.R(l0o))
            dd = self.nc.dram_tensor("dbg_s1", [128, CK * m1], BF16,
                                     kind="ExternalOutput")
            self.dma(dd.ap(), S1[:], [R(S1)], [Prog.R(dd)], tag="dbg")

        xc = self.sb("xc", [128, CK * TC], F32)
        xbf = self.sb("xbf", [128, CK * TC], BF16)
        sqbf = self.sb("sqbf", [128, CK * TC], BF16)
        xn = self.sb("xn", [128, CK * TC], BF16)
        srow = self.sb("srow", [1, 4 * TC], F32)
        wv_bf = self.sb("wv_bf", [128, CK * C], BF16)
        wc_bf = self.sb("wc_bf", [128, CK * C], BF16)
        for wdst, wsrc in ((wv_bf, wvT), (wc_bf, wcT)):
            self.dma(wdst[:], wsrc.ap().rearrange("(kc p) m -> p kc m", p=128),
                     [R(wsrc)], [R(wdst)], q="pool", tag="p5L")
        vbf = sqbf
        ybf = xn
        l0et = self.sb("l0et", [128, CK * H], BF16)
        for tci in range(NTC):
            t0 = tci * TC
            i0 = t0 // 2
            self.dma(xc[:], xres_r[:, :, t0:t0 + TC],
                     [R(xres, t0, t0 + TC)], [R(xc)], tag="p5X")
            layernorm_chunk("p5")
            for mc in range(CK):
                pt = self.nps()
                self.mmacc(pt[0:128, 0:TC],
                           [(wv_bf[:, kc * C + mc * 128:kc * C + (mc + 1) * 128],
                             xn[:, kc * TC:(kc + 1) * TC]) for kc in range(CK)],
                           [R(wv_bf), R(xn)], [R(pt, 0, TC)], tag="p5V")
                self.copy(vbf[:, mc * TC:(mc + 1) * TC], pt[0:128, 0:TC],
                          [R(pt, 0, TC)], [R(vbf, mc * TC, (mc + 1) * TC)],
                          tag="p5Ve")
            self.dma(l0et[:].rearrange("p (c t) -> p c t", c=CK),
                     l0e.ap().rearrange("c p t -> p c t")[:, :, i0 + 1:i0 + 1 + H],
                     [R(l0e, i0 + 1, i0 + 1 + H)], [R(l0et)], tag="p5Lo")
            le_f = lambda kc, i0=i0: S1[:, kc * m1 + i0:kc * m1 + i0 + H]
            lo_f = lambda kc: l0et[:, kc * H:(kc + 1) * H]
            zfi[0] += 1
            combine_block(le_f, lo_f, H, [R(S1), R(l0et)], "p5c")
            zf = zfs[zfi[0] % 2]

            def apply_rn(ps_sc, ps_sh, zf=zf, t_=rn_sets[zfi[0] % 2]["t"]):
                for c in range(CK):
                    self.tt(t_[:, 0:H], zf[:, c * H:(c + 1) * H],
                            ps_sc[0:128, 0:H], OP.mult,
                            [R(zf, c * H, (c + 1) * H), R(ps_sc, 0, H)],
                            [R(t_)], tag="p5a0")
                    self.tt(t_[:, 0:H], t_[:, 0:H], ps_sh[0:128, 0:H], OP.add,
                            [R(t_), R(ps_sh, 0, H)], [R(t_)], tag="p5a1")
                    self.tt(ybf[:, c * TC + 1:(c + 1) * TC:2], t_[:, 0:H],
                            vbf[:, c * TC + 1:(c + 1) * TC:2], OP.mult,
                            [R(t_), R(vbf, c * TC, (c + 1) * TC)],
                            [R(ybf, c * TC, (c + 1) * TC)], tag="p5a2")
            rangenorm(zf, H, apply_rn, "p5")
            for c in range(CK):
                self.tt(ybf[:, c * TC:(c + 1) * TC:2],
                        S1[:, c * m1 + i0:c * m1 + i0 + H],
                        vbf[:, c * TC:(c + 1) * TC:2], OP.mult,
                        [R(S1), R(vbf, c * TC, (c + 1) * TC)],
                        [R(ybf, c * TC, (c + 1) * TC)], tag="p5ye")
            for mc in range(CK):
                pt = self.nps()
                self.mmacc(pt[0:128, 0:TC],
                           [(wc_bf[:, kc * C + mc * 128:kc * C + (mc + 1) * 128],
                             ybf[:, kc * TC:(kc + 1) * TC]) for kc in range(CK)],
                           [R(wc_bf), R(ybf)], [R(pt, 0, TC)], tag="p5W")
                sl = slice(mc * TC, (mc + 1) * TC)
                self.tt(xc[:, sl], xc[:, sl], pt[0:128, 0:TC], OP.add,
                        [R(xc, mc * TC, (mc + 1) * TC), R(pt, 0, TC)],
                        [R(xc, mc * TC, (mc + 1) * TC)], tag="p5R")
            self.dma(xres_r[:, :, t0:t0 + TC], xc[:],
                     [R(xc)], [R(xres, t0, t0 + TC)], tag="p5S")
        self.free(wv_bf, wc_bf, l0et, zfs[0], zfs[1], S1,
                  owc[1], owc[2])
        for st_ in rn_sets:
            self.free(*st_.values())
        self.dbg_dump("dbg_p5", xres, [CK, 128, T], Prog.R(xres))

        weff = self.sb("weff2", [128, CK * C], BF16)
        self.dma(weff[:], weffs_d.ap(), [R(weffs_d)], [R(weff)], tag="p6L")
        ffo = self.sb("ffo", [128, CK * TC], F32)
        otok = None

        def out_convert(t0):
            for tb in range(TC // 128):
                tb0 = t0 + tb * 128
                for c in range(CK):
                    pt = self.nps()
                    self.tr(pt[0:128, 0:128],
                            xc[:, c * TC + tb * 128:c * TC + (tb + 1) * 128],
                            id128[:],
                            [R(xc, c * TC + tb * 128, c * TC + (tb + 1) * 128),
                             R(id128)],
                            [R(pt, 0, 128)], tag="pBt")
                    self.copy(otok[:, c * 128:(c + 1) * 128], pt[0:128, 0:128],
                              [R(pt, 0, 128)], [R(otok, c * 128, (c + 1) * 128)],
                              tag="pBtc")
                self.dma(out_d[tb0:tb0 + 128, :], otok[:], [R(otok)],
                         [R(out_d, tb0, tb0 + 128)], tag="pBS")

        def ffn_pass(ph, extra_add=None, emit_out=False, rev=False):
            tcs = range(NTC - 1, -1, -1) if rev else range(NTC)
            for tci in tcs:
                t0 = tci * TC
                self.dma(xc[:], xres_r[:, :, t0:t0 + TC],
                         [R(xres, t0, t0 + TC)], [R(xc)], tag=f"{ph}X")
                if extra_add is not None:
                    extra_add(t0)
                layernorm_chunk(ph)
                for mc in range(CK):
                    pt = self.nps()
                    self.mmacc(pt[0:128, 0:TC],
                               [(weff[:, kc * C + mc * 128:kc * C + (mc + 1) * 128],
                                 xn[:, kc * TC:(kc + 1) * TC]) for kc in range(CK)],
                               [R(weff), R(xn)], [R(pt, 0, TC)], tag=f"{ph}M")
                    sl = slice(mc * TC, (mc + 1) * TC)
                    self.acta(ffo[:, sl], pt[0:128, 0:TC],
                              fb_col[:, mc:mc + 1],
                              [R(pt, 0, TC), R(fb_col, mc, mc + 1)],
                              [R(ffo, mc * TC, (mc + 1) * TC)], tag=f"{ph}E")
                    self.tt(xc[:, sl], xc[:, sl], ffo[:, sl], OP.add,
                            [R(xc, mc * TC, (mc + 1) * TC),
                             R(ffo, mc * TC, (mc + 1) * TC)],
                            [R(xc, mc * TC, (mc + 1) * TC)], tag=f"{ph}R")
                if emit_out:
                    out_convert(t0)
                else:
                    self.dma(xres_r[:, :, t0:t0 + TC], xc[:],
                             [R(xc)], [R(xres, t0, t0 + TC)], tag=f"{ph}S")

        ffn_pass("p6")
        self.free(weff, ffo)
        self.dbg_dump("dbg_p6", xres, [CK, 128, T], Prog.R(xres))

        wix_bf = self.sb("wix_bf", [128, CK * DMY], BF16)
        wiz_bf = self.sb("wiz_bf", [128, CK * DMY], BF16)
        for wdst, wsrc in ((wix_bf, winxT), (wiz_bf, winzT)):
            self.dma(wdst[:], wsrc.ap().rearrange("(kc p) m -> p kc m", p=128),
                     [R(wsrc)], [R(wdst)], q="pool", tag="p7L")
        wx_bf = {}; cwf = {}; cbf = {}
        for p_ in ("f", "b"):
            wx_bf[p_] = self.sb(f"wx_bf{p_}", [128, DMK * 96], BF16)
            self.dma(wx_bf[p_][:],
                     mbw[p_]["wxT"].ap().rearrange("(kc p) m -> p kc m", p=128),
                     [R(mbw[p_]["wxT"])], [R(wx_bf[p_])], q="pool", tag="p7L")
            cwf[p_] = self.sb(f"cw_s{p_}", [128, DMK * 4], F32)
            self.dma(cwf[p_][:],
                     mbw[p_]["cw"].ap().rearrange("(dc p) j -> p dc j", p=128),
                     [R(mbw[p_]["cw"])], [R(cwf[p_])], tag="p7L", slow=True)
            cbf[p_] = self.sb(f"cb_s{p_}", [128, DMK], F32)
            self.dma(cbf[p_][:],
                     mbw[p_]["cb"].ap().rearrange("(dc p) -> p dc", p=128),
                     [R(mbw[p_]["cb"])], [R(cbf[p_])], tag="p7L", slow=True)
        xdbt = self.sb("xdbt", [80, TC], BF16)
        xdbtC = self.sb("xdbtC", [16, TC], BF16)
        PAD = 3
        XW = TC + 2 * PAD
        xzx = [self.sb(f"xzx{i}", [128, DMK * XW], BF16) for i in range(2)]
        for i in range(2):
            self.memset(xzx[i][:], 0.0, [], [R(xzx[i])], tag="p7ms")
        uacc = self.sb("uacc", [128, TC], F32)
        ubf = self.sb("ubf", [128, DMK * TC], BF16)
        szt = self.sb("szt", [128, TC], BF16)

        def conv_branch(p_, buf, t0):
            for dc in range(DMK):
                base = dc * XW
                dsl = slice(dc * TC, (dc + 1) * TC)
                for j in range(4):
                    off = j if p_ == "f" else 6 - j
                    src = xzx[buf][:, base + off:base + off + TC]
                    rd = [R(xzx[buf], base, base + XW),
                          R(cwf[p_], dc * 4 + j, dc * 4 + j + 1)]
                    if j == 0:
                        self.ts(uacc[:], src,
                                cwf[p_][:, dc * 4 + j:dc * 4 + j + 1], OP.mult,
                                rd, [R(uacc)], tag=f"p7c{p_}0")
                    else:
                        self.stt(uacc[:], src,
                                 cwf[p_][:, dc * 4 + j:dc * 4 + j + 1],
                                 uacc[:], OP.mult, OP.add,
                                 rd + [R(uacc)], [R(uacc)],
                                 tag=f"p7c{p_}1")
                self.act(ubf[:, dsl], uacc[:], AF.Silu,
                         [R(uacc), R(cbf[p_], dc, dc + 1)],
                         [R(ubf, dc * TC, (dc + 1) * TC)],
                         bias=cbf[p_][:, dc:dc + 1], tag=f"p7si{p_}")
            self.dma(u_sc[p_].ap().rearrange("c p t -> p c t")[:, :, t0:t0 + TC],
                     ubf[:], [R(ubf)], [R(u_sc[p_], t0, t0 + TC)], tag=f"p7uS{p_}")
            bi = 0 if p_ == "f" else 1
            for g, (m0, m1, dst) in enumerate(((0, 80, xdbt), (80, 96, xdbtC))):
                pt = self.nps()
                self.mmacc(pt[0:m1 - m0, 0:TC],
                           [(wx_bf[p_][:, kc * 96 + m0:kc * 96 + m1],
                             ubf[:, kc * TC:(kc + 1) * TC]) for kc in range(DMK)],
                           [R(wx_bf[p_]), R(ubf)], [R(pt, 0, TC)], tag=f"p7xd{p_}")
                self.copy(dst[0:m1 - m0, 0:TC], pt[0:m1 - m0, 0:TC],
                          [R(pt, 0, TC)], [R(dst)], tag=f"p7xe{p_}")
                self.dma(xdb_d[bi, m0:m1, t0:t0 + TC].rearrange("r t -> r t")
                         if False else xdb_d.ap()[bi, m0:m1, t0:t0 + TC],
                         dst[0:m1 - m0, 0:TC], [R(dst)],
                         [R(xdb_d, bi * T + t0, bi * T + t0 + TC)],
                         tag=f"p7cc{p_}")

        for tci in range(NTC):
            t0 = tci * TC
            buf = tci % 2
            self.dma(xc[:], xres_r[:, :, t0:t0 + TC],
                     [R(xres, t0, t0 + TC)], [R(xc)], tag="p7X")
            layernorm_chunk("p7")
            for mc in range(DMK):
                pt = self.nps()
                self.mmacc(pt[0:128, 0:TC],
                           [(wix_bf[:, kc * DMY + mc * 128:kc * DMY + (mc + 1) * 128],
                             xn[:, kc * TC:(kc + 1) * TC]) for kc in range(CK)],
                           [R(wix_bf), R(xn)], [R(pt, 0, TC)], tag="p7ix")
                self.copy(xzx[buf][:, mc * XW + PAD:mc * XW + PAD + TC],
                          pt[0:128, 0:TC], [R(pt, 0, TC)],
                          [R(xzx[buf], mc * XW + PAD, mc * XW + PAD + TC)],
                          tag="p7ixe")
                pt2 = self.nps()
                self.mmacc(pt2[0:128, 0:TC],
                           [(wiz_bf[:, kc * DMY + mc * 128:kc * DMY + (mc + 1) * 128],
                             xn[:, kc * TC:(kc + 1) * TC]) for kc in range(CK)],
                           [R(wiz_bf), R(xn)], [R(pt2, 0, TC)], tag="p7iz")
                self.act(szt[:], pt2[0:128, 0:TC], AF.Silu,
                         [R(pt2, 0, TC)], [R(szt)], tag="p7sz")
                self.dma(sz_sc.ap()[mc, :, t0:t0 + TC], szt[:],
                         [R(szt)], [R(sz_sc, t0, t0 + TC)], tag="p7szS")
            if tci > 0:
                for dc in range(DMK):
                    self.copy(xzx[buf][:, dc * XW:dc * XW + PAD],
                              xzx[1 - buf][:, dc * XW + TC:dc * XW + PAD + TC],
                              [R(xzx[1 - buf], dc * XW + TC, dc * XW + PAD + TC)],
                              [R(xzx[buf], dc * XW, dc * XW + PAD)], eng="dve",
                              tag="p7hp")
            conv_branch("f", buf, t0)
            if tci > 0:
                for dc in range(DMK):
                    self.copy(xzx[1 - buf][:, dc * XW + PAD + TC:dc * XW + XW],
                              xzx[buf][:, dc * XW + PAD:dc * XW + 2 * PAD],
                              [R(xzx[buf], dc * XW + PAD, dc * XW + 2 * PAD)],
                              [R(xzx[1 - buf], dc * XW + PAD + TC, dc * XW + XW)],
                              eng="dve", tag="p7tp")
                conv_branch("b", 1 - buf, t0 - TC)
            if tci == NTC - 1:
                for dc in range(DMK):
                    self.memset(xzx[buf][:, dc * XW + PAD + TC:dc * XW + XW], 0.0,
                                [], [R(xzx[buf], dc * XW + PAD + TC, dc * XW + XW)],
                                tag="p7zt")
                conv_branch("b", buf, t0)
        self.free(wix_bf, wiz_bf, xzx[0], xzx[1], uacc, szt, xdbt, xdbtC, ubf)
        for p_ in ("f", "b"):
            self.free(wx_bf[p_], cwf[p_], cbf[p_])
        self.free(xc, xbf, sqbf, xn, srow)

        if self.dbg:
            for p_ in ("f", "b"):
                self.dbg_dump(f"dbg_u{p_}", u_sc[p_], [0], Prog.R(u_sc[p_]))
            self.dbg_dump("dbg_sz", sz_sc, [0], Prog.R(sz_sc))
            self.dbg_dump("dbg_xdb", xdb_d, [0], Prog.R(xdb_d))

        dtt = [self.sb(f"dtt{i}", [64, TC], BF16) for i in range(2)]
        xbc = [self.sb(f"xbc{i}", [48, TC], BF16) for i in range(2)]

        r80 = self.sb("r80", [128, TC], BF16)
        oneh_bf = self.sb("oneh_bf", [48, NST * 128], BF16)
        self.dma(oneh_bf[:], onehot.ap(), [R(onehot)], [R(oneh_bf)],
                 q="pool", tag="p9L")
        wo_bf = self.sb("wo_bf", [128, DMK * C], BF16)
        self.dma(wo_bf[:], woutT.ap().rearrange("(kc p) m -> p kc m", p=128),
                 [R(woutT)], [R(wo_bf)], q="pool", tag="p9L")
        wdt_bf = self.sb("wdt_bf", [64, DMY], BF16)
        Asb = self.sb("Asb", [128, DMK * NST], F32)
        bdtc80 = {}; dvc = {}
        for p_ in ("f", "b"):
            bdtc80[p_] = self.sb(f"bdtc80_{p_}", [128, DMK], F32)
            self.dma(bdtc80[p_][:],
                     mbw[p_]["bdt"].ap().rearrange("(dc p) -> p dc", p=128),
                     [R(mbw[p_]["bdt"])], [R(bdtc80[p_])], tag="p9L", slow=True)
            self.ts(bdtc80[p_][:], bdtc80[p_][:], -1.0, OP.mult,
                    [R(bdtc80[p_])], [R(bdtc80[p_])], s2=80.0, op1=OP.add,
                    tag="p9b80")
            dvc[p_] = self.sb(f"dvc_{p_}", [128, DMK], F32)
            self.dma(dvc[p_][:],
                     mbw[p_]["dv"].ap().rearrange("(dc p) -> p dc", p=128),
                     [R(mbw[p_]["dv"])], [R(dvc[p_])], tag="p9L", slow=True)

        def load_branch(p_, wdt_only):
            self.dma(wdt_bf[:], mbw[p_]["wdtT"].ap(), [R(mbw[p_]["wdtT"])],
                     [R(wdt_bf)], q="pool", tag="p9Lw")
            if wdt_only:
                return
            self.dma(Asb[:],
                     mbw[p_]["alog"].ap().rearrange("(dc p) n -> p dc n", p=128),
                     [R(mbw[p_]["alog"])], [R(Asb)], tag="p9LA", slow=True)
            self.act(Asb[:], Asb[:], AF.Exp, [R(Asb)], [R(Asb)], tag="p9A0")
            self.ts(Asb[:], Asb[:], -1.0, OP.mult, [R(Asb)], [R(Asb)],
                    tag="p9A1")

        def load_asb(p_):
            self.dma(Asb[:],
                     mbw[p_]["alog"].ap().rearrange("(dc p) n -> p dc n", p=128),
                     [R(mbw[p_]["alog"])], [R(Asb)], tag="p9LA", slow=True)
            self.act(Asb[:], Asb[:], AF.Exp, [R(Asb)], [R(Asb)], tag="p9A0")
            self.ts(Asb[:], Asb[:], -1.0, OP.mult, [R(Asb)], [R(Asb)],
                    tag="p9A1")

        DC8 = DMK // 2
        vdt = [self.sb(f"vdt{j}", [128, DC8 * TC], BF16) for j in range(2)]
        dub = [self.sb(f"dub{j}", [128, DC8 * TC], BF16) for j in range(2)]
        ub8 = [self.sb(f"ub8{j}", [128, DC8 * TC], BF16) for j in range(2)]
        sz8 = self.sb("sz8", [128, DC8 * TC], BF16)
        da2 = self.sb("da2", [128, 4 * TC], BF16)
        db2 = self.sb("db2", [128, 4 * TC], BF16)
        h2 = self.sb("h2", [128, 4 * TC], BF16)
        y_dc = [self.sb(f"y_dc{j}", [128, DC8 * TC], BF16) for j in range(2)]
        carry = self.sb("carry", [128, DMK * NST], F32)
        opc = self.sb("opc", [128, CK * TC], BF16)
        bba = [self.sb(f"bba{j}", [128, NST * TC], BF16) for j in range(2)]
        cba = [self.sb(f"cba{j}", [128, NST * TC], BF16) for j in range(2)]

        blocks = []
        for p_ in ("f", "b"):
            tcs = range(NTC) if p_ == "f" else range(NTC - 1, -1, -1)
            for tci in tcs:
                for dh in (0, 1):
                    blocks.append((p_, tci, dh))
        NBLK = len(blocks)
        NIT = NST * DC8
        pend_out = {}

        def mk_prep(k):
            p_, tci, dh = blocks[k]
            fwd = p_ == "f"
            bi = 0 if fwd else 1
            t0 = tci * TC
            kb = k % 2
            kt = (k // 2) % 2
            dco = dh * DC8
            cl = []
            if k == NBLK // 2:
                cl.append(lambda: load_branch("b", True))
            if dh == 0:
                def ldx(r0, r1, dst, d0):
                    self.dma(dst[kt][d0:d0 + r1 - r0, 0:TC],
                             xdb_d.ap()[bi, r0:r1, t0:t0 + TC],
                             [R(xdb_d, bi * T + t0, bi * T + t0 + TC)],
                             [R(dst[kt])], q="act", tag="p9xl")
                cl.append(lambda: ldx(0, 64, dtt, 0))
                cl.append(lambda: ldx(64, 80, xbc, 0))
                cl.append(lambda: ldx(80, 96, xbc, 32))
                def bc(n, ro, dst):
                    ps = self.nps()
                    self.mmacc(ps[0:128, 0:TC],
                               [(oneh_bf[ro:ro + NST, n * 128:(n + 1) * 128],
                                 xbc[kt][ro:ro + NST, 0:TC])],
                               [R(oneh_bf), R(xbc[kt])],
                               [R(ps, 0, TC)], tag="p9bb")
                    self.copy(dst[kt][:, n * TC:(n + 1) * TC], ps[0:128, 0:TC],
                              [R(ps, 0, TC)],
                              [R(dst[kt], n * TC, (n + 1) * TC)], tag="p9be")
                for n in range(NST):
                    cl.append(lambda n=n: bc(n, 0, bba))
                    cl.append(lambda n=n: bc(n, 32, cba))
            def vda(mi):
                mc = dco + mi
                pt = self.nps()
                self.mmacc(pt[0:128, 0:TC],
                           [(wdt_bf[0:64, mc * 128:(mc + 1) * 128],
                             dtt[kt][0:64, 0:TC])],
                           [R(wdt_bf), R(dtt[kt])], [R(pt, 0, TC)], tag="p9dt")
                self.act(r80[:], pt[0:128, 0:TC], AF.Relu,
                         [R(pt, 0, TC), R(bdtc80[p_], mc, mc + 1)], [R(r80)],
                         bias=bdtc80[p_][:, mc:mc + 1], scale=-1.0, tag="p9r8")
            def vdb(mi):
                self.act(r80[:], r80[:], AF.Exp, [R(r80)], [R(r80)],
                         bias=80.0, scale=-1.0, tag="p9e8")
                self.act(vdt[kb][:, mi * TC:(mi + 1) * TC], r80[:], AF.Ln,
                         [R(r80)], [R(vdt[kb], mi * TC, (mi + 1) * TC)],
                         bias=1.0, tag="p9l8")
            for mi in range(DC8):
                cl.append(lambda mi=mi: vda(mi))
                cl.append(lambda mi=mi: vdb(mi))
            def u8l():
                self.dma(ub8[kb][:].rearrange("p (c t) -> p c t", c=DC8),
                         u_sc[p_].ap()[dco:dco + DC8, :, t0:t0 + TC]
                         .rearrange("c p t -> p c t"),
                         [R(u_sc[p_], t0, t0 + TC)], [R(ub8[kb])],
                         q="act", tag="p9u")
            cl.append(u8l)
            cl.append(lambda: self.tt(dub[kb][:], vdt[kb][:], ub8[kb][:],
                                      OP.mult, [R(vdt[kb]), R(ub8[kb])],
                                      [R(dub[kb])], tag="p9du"))
            return cl

        def mk_tail(k):
            p_, tci, dh = blocks[k]
            fwd = p_ == "f"
            t0 = tci * TC
            cl = []
            for mc, ps in pend_out[k]:
                sl = slice(mc * TC, (mc + 1) * TC)
                if fwd and dh == 0:
                    cl.append(lambda sl=sl, ps=ps, mc=mc: self.copy(
                        opc[:, sl], ps[0:128, 0:TC], [R(ps, 0, TC)],
                        [R(opc, sl.start, sl.stop)], tag="p9oe"))
                else:
                    cl.append(lambda sl=sl, ps=ps, mc=mc: self.tt(
                        opc[:, sl], opc[:, sl], ps[0:128, 0:TC], OP.add,
                        [R(opc, sl.start, sl.stop), R(ps, 0, TC)],
                        [R(opc, sl.start, sl.stop)], tag="p9oa"))
            if dh == 1:
                cl.append(lambda: self.dma(
                    outp.ap().rearrange("c p t -> p c t")[:, :, t0:t0 + TC],
                    opc[:], [R(opc)], [R(outp, t0, t0 + TC)], tag="p9oS"))
            return cl

        load_branch("f", False)
        for c in mk_prep(0):
            c()

        for k, (p_, tci, dh) in enumerate(blocks):
            fwd = p_ == "f"
            t0 = tci * TC
            kb = k % 2
            kt = (k // 2) % 2
            dco = dh * DC8
            vd, du, u8 = vdt[kb], dub[kb], ub8[kb]
            ydc = y_dc[kb]
            sched = {}

            def add(g, c):
                sched.setdefault(g, []).append(c)

            add(2, lambda: self.dma(
                sz8[:].rearrange("p (c t) -> p c t", c=DC8),
                sz_sc.ap()[dco:dco + DC8, :, t0:t0 + TC]
                .rearrange("c p t -> p c t"),
                [R(sz_sc, t0, t0 + TC)], [R(sz8)], q="act", tag="p9sz"))
            def ydf(mi):
                dc = dco + mi
                dsl = slice(mi * TC, (mi + 1) * TC)
                self.stt(ydc[:, dsl], u8[:, dsl], dvc[p_][:, dc:dc + 1],
                         ydc[:, dsl], OP.mult, OP.add,
                         [R(u8, mi * TC, (mi + 1) * TC), R(dvc[p_], dc, dc + 1),
                          R(ydc, mi * TC, (mi + 1) * TC)],
                         [R(ydc, mi * TC, (mi + 1) * TC)], tag="p9yd")
            def ygf(mi):
                dsl = slice(mi * TC, (mi + 1) * TC)
                self.tt(ydc[:, dsl], ydc[:, dsl], sz8[:, dsl], OP.mult,
                        [R(ydc, mi * TC, (mi + 1) * TC),
                         R(sz8, mi * TC, (mi + 1) * TC)],
                        [R(ydc, mi * TC, (mi + 1) * TC)], tag="p9yg")
            for mi in range(DC8 - 1):
                add(mi * 16 + 18, lambda mi=mi: ydf(mi))
                add(mi * 16 + 19, lambda mi=mi: ygf(mi))
            if k >= 1:
                for j, c in enumerate(mk_tail(k - 1)):
                    add(18 + 2 * j, c)
            if (not fwd) and dh == 0:
                add(40, lambda: self.dma(
                    opc[:].rearrange("p (c t) -> p c t", c=CK),
                    outp.ap().rearrange("c p t -> p c t")[:, :, t0:t0 + TC],
                    [R(outp, t0, t0 + TC)], [R(opc)], tag="p9oL"))
            if k + 1 < NBLK:
                slots = list(range(19, 34, 2)) + \
                        [x for x in range(36, 128) if x != 40]
                for c, sl_ in zip(mk_prep(k + 1), slots):
                    add(sl_, c)

            if k == 0 or k == NBLK // 2:
                self.memset(carry[:], 0.0, [], [R(carry)], tag="p9cm")
                if k == NBLK // 2:
                    load_asb("b")

            def it_of(i):
                mi, n = divmod(i, NST)
                return n, mi, (dco + mi) * NST + n

            def sl4(i):
                par = i % 4
                return par * TC, (par + 1) * TC

            def emit_exp(i):
                n, mi, ci = it_of(i)
                blo, bhi = sl4(i)
                self.act(da2[:, blo:bhi], vd[:, mi * TC:(mi + 1) * TC], AF.Exp,
                         [R(vd, mi * TC, (mi + 1) * TC), R(Asb, ci, ci + 1)],
                         [R(da2, blo, bhi)],
                         scale=Asb[:, ci:ci + 1], tag="p9da")

            def emit_db(i):
                n, mi, ci = it_of(i)
                blo, bhi = sl4(i)
                self.tt(db2[:, blo:bhi], du[:, mi * TC:(mi + 1) * TC],
                        bba[kt][:, n * TC:(n + 1) * TC], OP.mult,
                        [R(du, mi * TC, (mi + 1) * TC),
                         R(bba[kt], n * TC, (n + 1) * TC)],
                        [R(db2, blo, bhi)], tag="p9db")

            def emit_scan(i):
                n, mi, ci = it_of(i)
                blo, bhi = sl4(i)
                if fwd:
                    o_ap = h2[:, blo:bhi]
                    a_ap = da2[:, blo:bhi]
                    b_ap = db2[:, blo:bhi]
                else:
                    o_ap = h2[:, bhi - 1:blo - 1 if blo else None:-1]
                    a_ap = da2[:, bhi - 1:blo - 1 if blo else None:-1]
                    b_ap = db2[:, bhi - 1:blo - 1 if blo else None:-1]
                self.scan(o_ap, a_ap, b_ap, carry[:, ci:ci + 1],
                          [R(da2, blo, bhi), R(db2, blo, bhi),
                           R(carry, ci, ci + 1)],
                          [R(h2, blo, bhi)], tag="p9sc")

            def emit_cc(i):
                n, mi, ci = it_of(i)
                blo, bhi = sl4(i)
                src_col = h2[:, bhi - 1:bhi] if fwd else h2[:, blo:blo + 1]
                self.copy(carry[:, ci:ci + 1], src_col,
                          [R(h2, blo, bhi)], [R(carry, ci, ci + 1)],
                          eng="act", tag="p9cc")

            def emit_y1(i):
                n, mi, ci = it_of(i)
                blo, bhi = sl4(i)
                if n == 0:
                    self.tt(ydc[:, mi * TC:(mi + 1) * TC], h2[:, blo:bhi],
                            cba[kt][:, n * TC:(n + 1) * TC], OP.mult,
                            [R(h2, blo, bhi),
                             R(cba[kt], n * TC, (n + 1) * TC)],
                            [R(ydc, mi * TC, (mi + 1) * TC)], tag="p9y0")
                else:
                    self.tt(da2[:, blo:bhi], h2[:, blo:bhi],
                            cba[kt][:, n * TC:(n + 1) * TC], OP.mult,
                            [R(h2, blo, bhi),
                             R(cba[kt], n * TC, (n + 1) * TC)],
                            [R(da2, blo, bhi)], tag="p9y1")

            def emit_y2(i):
                n, mi, ci = it_of(i)
                blo, bhi = sl4(i)
                if n == 0:
                    return
                self.tt(ydc[:, mi * TC:(mi + 1) * TC],
                        ydc[:, mi * TC:(mi + 1) * TC], da2[:, blo:bhi], OP.add,
                        [R(ydc, mi * TC, (mi + 1) * TC), R(da2, blo, bhi)],
                        [R(ydc, mi * TC, (mi + 1) * TC)], tag="p9y2")

            emit_exp(0)
            emit_exp(1)
            emit_db(0)
            for g in range(NIT):
                if g + 2 < NIT:
                    emit_exp(g + 2)
                emit_scan(g)
                if g >= 1:
                    emit_y1(g - 1)
                if g + 1 < NIT:
                    emit_db(g + 1)
                if g >= 1:
                    emit_y2(g - 1)
                emit_cc(g)
                for c in sched.pop(g, ()):
                    c()
            emit_y1(NIT - 1)
            emit_y2(NIT - 1)
            for c in sorted(sched):
                for f in sched[c]:
                    f()
            ydf(DC8 - 1)
            ygf(DC8 - 1)
            pend_out[k] = []
            for mc in range(CK):
                pt = self.nps()
                self.mmacc(pt[0:128, 0:TC],
                           [(wo_bf[:, (dco + kc) * C + mc * 128:
                                   (dco + kc) * C + (mc + 1) * 128],
                             ydc[:, kc * TC:(kc + 1) * TC])
                            for kc in range(DC8)],
                           [R(wo_bf), R(ydc)], [R(pt, 0, TC)], tag="p9o")
                pend_out[k].append((mc, pt))

        for c in mk_tail(NBLK - 1):
            c()

        self.free(vdt[0], vdt[1], dub[0], dub[1], ub8[0], ub8[1], sz8,
                  da2, db2, h2, y_dc[0], y_dc[1], carry, wo_bf,
                  bba[0], bba[1], cba[0], cba[1], r80, wdt_bf, Asb)
        for p_ in ("f", "b"):
            self.free(bdtc80[p_], dvc[p_])
        self.free(dtt[0], dtt[1], xbc[0], xbc[1])

        self.dbg_dump("dbg_outp", outp, [CK, 128, T], Prog.R(outp))

        def mamba_add(t0):
            self.dma(opc[:].rearrange("p (c t) -> p c t", c=CK),
                     outp.ap().rearrange("c p t -> p c t")[:, :, t0:t0 + TC],
                     [R(outp, t0, t0 + TC)], [R(opc)], tag="pBo")
            self.stt(xc[:], opc[:], 0.5, xc[:], OP.mult, OP.add,
                     [R(opc), R(xc)], [R(xc)], tag="pBa")

        xc = self.sb("xc", [128, CK * TC], F32)
        xbf = self.sb("xbf", [128, CK * TC], BF16)
        sqbf = self.sb("sqbf", [128, CK * TC], BF16)
        xn = self.sb("xn", [128, CK * TC], BF16)
        srow = self.sb("srow", [1, 4 * TC], F32)
        ffo = self.sb("ffo", [128, CK * TC], F32)
        otok = self.sb("otok", [128, C], F32)
        weff = self.sb("weff3", [128, CK * C], BF16)
        self.dma(weff[:], weffs_d.ap(), [R(weffs_d)], [R(weff)], tag="pBL")
        ffn_pass("pB", extra_add=mamba_add, emit_out=True, rev=True)
        return self


def build_program(cfg: Cfg, dbg=False):
    b = Builder(cfg, dbg=dbg)
    b.build()
    b.p.finalize(b.es)
    return b


_CFG = Cfg()


def _prep_inputs(inputs, cfg: Cfg, core):
    global CONSTS
    import jax
    import jax.numpy as jnp
    bidx, h = core // 2, core % 2
    C, T, DMY, DI = cfg.C, cfg.T, cfg.DMY, cfg.DI
    f32 = lambda a: np.ascontiguousarray(np.asarray(a, dtype=np.float32))
    tp = lambda a: np.ascontiguousarray(np.asarray(a, dtype=np.float32).T)
    dsl = slice(0, DMY)
    cpu = jax.devices("cpu")[0]
    with jax.default_device(cpu):
        v1 = np.asarray(jax.random.normal(jax.random.key(101), (C, 1),
                                          dtype=jnp.float32))
        v2 = np.asarray(jax.random.normal(jax.random.key(102), (C, 1),
                                          dtype=jnp.float32))
    NST = cfg.NST
    oneh = np.zeros((48, NST * 128), np.float32)
    for n in range(NST):
        oneh[n, n * 128:(n + 1) * 128] = 1.0
        oneh[32 + n, n * 128:(n + 1) * 128] = 1.0
    W_in = f32(inputs["L0_attn_b_W_in"])
    d = {
        "x_full": f32(inputs["L0_attn_a_x"][bidx]),
        "identv": f32(inputs["L0_attn_a_identity"]),
        "v1c": f32(v1), "v2c": f32(v2),
        "p1r": f32(inputs["L0_attn_a_P1"]), "p1c": tp(inputs["L0_attn_a_P1"]),
        "p2r": f32(inputs["L0_attn_a_P2"]), "p2c": tp(inputs["L0_attn_a_P2"]),
        "wqT": tp(inputs["L0_attn_a_Wq"]), "wvT": tp(inputs["L0_attn_a_Wv"]),
        "wcT": tp(inputs["L0_attn_a_Wc"]),
        "ffn_wT": tp(inputs["L0_ffn_W"]), "ffn_b": f32(inputs["L0_ffn_b"]),
        "ffn_da": f32(inputs["L0_ffn_down_aux"]),
        "ffn_ldT": tp(inputs["L0_ffn_lora_down"]),
        "ffn_luT": tp(inputs["L0_ffn_lora_up"]),
        "ffn_uaT": tp(inputs["L0_ffn_up_aux"]),
        "onehot": oneh,
        "cstv": np.tile(np.array([CONSTS], dtype=np.float32), (128, 1)),
        "ident128": np.eye(128, dtype=np.float32),
        "winxT": tp(W_in[dsl]),
        "winzT": tp(W_in[DI + dsl.start:DI + dsl.stop]),
        "woutT": tp(f32(inputs["L0_attn_b_W_out"])[:, dsl]),
    }
    for p_, sfx in (("f", ""), ("b", "_b")):
        cw = f32(inputs[f"L0_attn_b_conv_w{sfx}"])[dsl]
        d[f"cw_{p_}"] = np.ascontiguousarray(cw[:, 0, :])
        d[f"cb_{p_}"] = f32(inputs[f"L0_attn_b_conv_b{sfx}"])[dsl]
        d[f"wxT_{p_}"] = tp(f32(inputs[f"L0_attn_b_Wx{sfx}"])[:, dsl])
        d[f"wdtT_{p_}"] = tp(f32(inputs[f"L0_attn_b_Wdt{sfx}"])[dsl])
        d[f"bdt_{p_}"] = f32(inputs[f"L0_attn_b_bdt{sfx}"])[dsl]
        alog_key = "L0_attn_b_A_log" if p_ == "f" else "L0_attn_b_A_b_log"
        d[f"alog_{p_}"] = f32(inputs[alog_key])[dsl]
        d[f"dv_{p_}"] = f32(inputs[f"L0_attn_b_D{sfx}"])[dsl]
    return d


def kernel(**inputs):
    cfg = _CFG
    b = build_program(cfg)
    in_maps = [_prep_inputs(inputs, cfg, core) for core in range(8)]
    res = run_bass_kernel_spmd(b.nc, in_maps, core_ids=list(range(8)))
    outs = [res.results[2 * bi]["out"] for bi in range(4)]
    return np.stack(outs, 0).astype(np.float32)
